# version 08
# baseline (speedup 1.0000x reference)
import math
from contextlib import ExitStack

import numpy as np
import ml_dtypes

import concourse.bass as bass
import concourse.mybir as mybir
from concourse.bass_utils import run_bass_kernel_spmd

F32 = mybir.dt.float32
BF16 = mybir.dt.bfloat16
AF = mybir.ActivationFunctionType
ALU = mybir.AluOpType
AX = mybir.AxisListType

S = 4096
D = 4096
H = 32
HD = 128
E = 64
NTOK = 1024
NB = 8
BLK = 512
HL = 8
EPS = 1e-5
ISQ = 1.0 / math.sqrt(HD)


def owner(j):
    return j if j < 4 else 7 - j


def half(j):
    return 0 if j < 4 else 1


def col_of(j):
    return owner(j) * 1024 + half(j) * 512


class Prog:
    ENG = ("tensor", "vector", "scalar", "gpsimd", "sync")

    def __init__(self):
        self.ops = {e: [] for e in self.ENG}
        self.cnt = {}
        self.clock = {e: {} for e in self.ENG}
        self.snaps = {}
        self.wr = {}
        self.rd = {}
        self.stream_q = {}
        self.sems = set(self.ENG)

    @staticmethod
    def _key(x):
        if isinstance(x, str):
            return x, False
        return x.name, str(x.space).upper().endswith("PSUM")

    def _need(self, eng, marker):
        sem, val = marker
        if sem == "tensor" and eng == "tensor":
            return False
        return self.clock[eng].get(sem, 0) < val

    def _observe(self, eng, marker):
        sem, val = marker
        c = self.clock[eng]
        c[sem] = max(c.get(sem, 0), val)
        for s, v in self.snaps.get(marker, {}).items():
            c[s] = max(c.get(s, 0), v)

    def _waits_for(self, eng, rds, wrs):
        needs = {}

        def add(m):
            if m is not None and self._need(eng, m):
                s, v = m
                needs[s] = max(needs.get(s, 0), v)

        for k in rds:
            add(self.wr.get(k))
        for k in wrs:
            add(self.wr.get(k))
            for m in self.rd.get(k, ()):
                add(m)
        waits = sorted(needs.items())
        for m in waits:
            self._observe(eng, m)
        return waits

    def _commit(self, marker, rds, wrs):
        for k in rds:
            self.rd.setdefault(k, []).append(marker)
        for k in wrs:
            self.wr[k] = marker
            self.rd[k] = []

    def _split(self, rd, wr):
        rds, wrs = [], []
        for x in rd:
            k, is_psum = self._key(x)
            (wrs if is_psum else rds).append(k)
        for x in wr:
            k, _ = self._key(x)
            wrs.append(k)
        return rds, wrs

    def op(self, eng, emit, rd=(), wr=(), tag=""):
        rds, wrs = self._split(rd, wr)
        waits = self._waits_for(eng, rds, wrs)
        self.cnt[eng] = self.cnt.get(eng, 0) + 1
        marker = (eng, self.cnt[eng])
        self.snaps[marker] = dict(self.clock[eng]) | {eng: self.cnt[eng]}
        self._commit(marker, rds, wrs)
        self.ops[eng].append((waits, emit, eng, 1, tag))
        return marker

    def dma(self, queue, stream, emit, rd=(), wr=(), tag=""):
        assert self.stream_q.setdefault(stream, queue) == queue, (
            f"stream {stream} used from two queues")
        R = 2 if queue == "gpsimd" else 4
        k = self.rr = getattr(self, "rr", {})
        n0 = k.get(stream, 0)
        k[stream] = n0 + 1
        stream = f"{stream}{n0 % R}"
        self.sems.add(stream)
        rds, wrs = self._split(rd, wr)
        prev_val = self.cnt.get(stream, 0)
        needs = {}
        if prev_val and self.clock[queue].get(stream, 0) < prev_val:
            needs[stream] = prev_val
        waits = self._waits_for(queue, rds, wrs)
        for s, v in waits:
            needs[s] = max(needs.get(s, 0), v)
        waits = sorted(needs.items())
        for m in waits:
            self._observe(queue, m)
        n = prev_val + 16
        self.cnt[stream] = n
        marker = (stream, n)
        prev = self.snaps.get((stream, n - 16), {})
        snap = dict(prev)
        for s, v in (self.clock[queue] | {stream: n}).items():
            snap[s] = max(snap.get(s, 0), v)
        self.snaps[marker] = snap
        self._commit(marker, rds, wrs)
        self.ops[queue].append((waits, emit, stream, 16, tag))
        return marker

    def coll(self, emit, rd=(), wr=(), tag="coll"):
        self.sems.add("coll")
        rds, wrs = self._split(rd, wr)
        waits = self._waits_for("gpsimd", rds, wrs)
        n = self.cnt.get("coll", 0) + 1
        self.cnt["coll"] = n
        marker = ("coll", n)
        snap = dict(self.snaps.get(("coll", n - 1), {}))
        for s, v in (self.clock["gpsimd"] | {"coll": n}).items():
            snap[s] = max(snap.get(s, 0), v)
        self.snaps[marker] = snap
        self._commit(marker, rds, wrs)
        self.ops["gpsimd"].append((waits, emit, "coll", 1, tag))
        return marker

    def barrier(self, tag="barrier"):
        targets = [(s, v) for s, v in self.cnt.items()
                   if not s.startswith(("wcast", "coll", "gx"))]
        for eng in self.ENG:
            waits = []
            for m in targets:
                if m[0] != eng and self._need(eng, m):
                    waits.append(m)
                    self._observe(eng, m)
            if waits:
                self.ops[eng].append((sorted(waits), None, eng, 0, tag))

    def emit_all(self, nc, sems, val_hook=None):
        def body_for(en):
            def body(eng):
                V = {}
                if en in ("sync", "gpsimd") and val_hook is not None:
                    V.update(val_hook(eng))
                for waits, emit, inc_sem, inc_val, tag in self.ops[en]:
                    for s, v in waits:
                        eng.wait_ge(sems[s], v)
                    if emit is not None:
                        inst = emit(eng, V)
                        if inc_val:
                            inst.then_inc(sems[inc_sem], inc_val)
            return body

        with nc.Block() as block:
            for en in self.ENG:
                if self.ops[en]:
                    getattr(block, en)(body_for(en))


def build_nc(n_cores=8, phase_limit=99):
    nc = bass.Bass("TRN2", target_bir_lowering=False)
    P = Prog()

    groups = [[0, 1, 2, 3], [4, 5, 6, 7]] if n_cores == 8 else [[0, 1, 2, 3]]

    x0 = nc.dram_tensor("x0", [NTOK, D], F32, kind="ExternalInput").ap()
    wq = nc.dram_tensor("wq", [D, D], F32, kind="ExternalInput").ap()
    wk = nc.dram_tensor("wk", [D, D], F32, kind="ExternalInput").ap()
    wv = nc.dram_tensor("wv", [D, D], F32, kind="ExternalInput").ap()
    wo = nc.dram_tensor("wo", [D, D], F32, kind="ExternalInput").ap()
    wr = nc.dram_tensor("wr", [E, D], F32, kind="ExternalInput").ap()
    rb = nc.dram_tensor("rb", [1, E], F32, kind="ExternalInput").ap()
    eb = nc.dram_tensor("eb", [E, D], F32, kind="ExternalInput").ap()
    identb_d = nc.dram_tensor("identb", [128, 128], BF16, kind="ExternalInput").ap()
    identf_d = nc.dram_tensor("identf", [128, 128], F32, kind="ExternalInput").ap()
    cos_d = nc.dram_tensor("cosT", [128, NTOK], F32, kind="ExternalInput").ap()
    sin_d = nc.dram_tensor("sinT", [128, NTOK], F32, kind="ExternalInput").ap()
    dmask_d = nc.dram_tensor("dmask", [128, 4 * BLK], BF16, kind="ExternalInput").ap()
    out_d = nc.dram_tensor("out", [NTOK, D], F32, kind="ExternalOutput").ap()

    x1d = nc.dram_tensor("x1d", [NTOK, D], F32).ap()
    x2d = nc.dram_tensor("x2d", [NTOK, D], F32).ap()
    kvq_in = nc.dram_tensor("kvq_in", [3 * D, NTOK], BF16).ap()
    sec_out = [nc.dram_tensor(f"sec_out{s}", [4 * D, NTOK], BF16).ap()
               for s in range(3)]
    sec_me = [nc.dram_tensor(f"sec_me{s}", [D, NTOK], BF16).ap()
              for s in range(3)]
    o_in = nc.dram_tensor("o_in", [HL * HD, S], BF16).ap()
    o_out = nc.dram_tensor("o_out", [4 * HL * HD, S], BF16).ap()

    with ExitStack() as ctx:
        en = ctx.enter_context

        sb = {}

        def T(name, shape, dtype):
            sb[name] = en(nc.sbuf_tensor(name, shape, dtype))
            return sb[name]

        identb = T("s_identb", [128, 128], BF16)
        identf = T("s_identf", [128, 128], F32)
        cosT = T("s_cosT", [128, NTOK], F32)
        sinT = T("s_sinT", [128, NTOK], F32)
        dmask = T("s_dmask", [128, 4 * BLK], BF16)
        ones1 = T("ones1", [1, 128], F32)
        WrT = T("WrT", [128, 32 * E], BF16)
        rb1 = T("rb1", [1, E], F32)
        rbb = T("rbb", [128, E], F32)
        ebb = T("ebb", [E, D], BF16)
        stat = {k: T(k, [128, 8], F32) for k in
                ("mean", "ssq", "varp", "lnv", "rstd", "mx", "nmx", "sexp", "rse",
                 "wsum")}
        ssq8 = T("ssq8", [128, 64], F32)
        lgs = [T(f"lgs{i}", [128, E], F32) for i in range(2)]
        exps = [T(f"exps{i}", [128, E], F32) for i in range(2)]
        scrs = [T(f"scrs{i}", [128, E], F32) for i in range(2)]
        top8 = [T(f"top8{i}", [128, 8], F32) for i in range(2)]
        zap = [T(f"zap{i}", [128, E], F32) for i in range(2)]
        wden = [T(f"wden{i}", [128, E], F32) for i in range(2)]
        wdb = [T(f"wdb{i}", [128, E], BF16) for i in range(2)]
        wdT = [T(f"wdT{i}", [E, 128], BF16) for i in range(2)]
        sums = T("sums", [128, 4], F32)
        sblk = T("sblk", [128, 8], F32)
        rs4 = T("rs4", [128, 4], F32)
        z2fm = T("z2fm", [128, 2 * 32 * BLK], BF16)

        ARENA = 94 * 1024
        arena = en(nc.sbuf_tensor("arena", [128, ARENA], mybir.dt.uint8))
        arena_addr = nc.lookup_mloc(arena).addr
        _goff = {}

        def A(group, name, shape, dtype):
            per = 1
            for s in shape[1:]:
                per *= s
            per *= mybir.dt.size(dtype)
            off = (_goff.get(group, 0) + 31) // 32 * 32
            _goff[group] = off + per
            assert _goff[group] <= ARENA, (group, name, _goff[group])
            h = nc.virtual_sbuf_tensor(name, list(shape), dtype)
            nc._tensor(h.name, h.shape, h.dtype, type="SB")
            mloc = nc.lookup_mloc(h)
            mloc.allocated = True
            mloc.addr = arena_addr + off
            sb[name] = h
            return h

        wrb = A("g0", "wrb", [E, D], BF16)

        wstripf = [A("g2", f"wstripf{i}", [128, D], F32) for i in range(2)]
        wstrip = [A("g2", f"wstrip{i}", [128, D], BF16) for i in range(2)]
        _goff["g4"] = _goff["g2"]

        xa = [A("g1", f"xa{i}", [128, D], F32) for i in range(2)]
        zc = [A("g1", f"zc{i}", [128, D], F32) for i in range(2)]
        zb = A("g1", "zb", [128, D], BF16)
        zfm = A("g1", "zfm", [128, D], BF16)
        zws = [A("g1", f"zws{i}", [128, BLK], F32) for i in range(2)]

        qraw = [A("g2", f"qraw{i}", [128, BLK], F32) for i in range(2)]
        rr1 = [A("g2", f"rr1{i}", [128, BLK], F32) for i in range(2)]
        rr2 = [A("g2", f"rr2{i}", [128, BLK], F32) for i in range(2)]
        rob = [A("g2", f"rob{i}", [128, BLK], BF16) for i in range(2)]
        vtmp = [A("g2", f"vtmp{i}", [128, BLK], BF16) for i in range(2)]

        Kh = [A("g3", f"Kh{i}", [128, S], BF16) for i in range(2)]
        Vf = [A("g3", f"Vf{i}", [128, S], BF16) for i in range(2)]
        Qh = [A("g3", f"Qh{i}", [128, S], BF16) for i in range(2)]
        Vt = [A("g3", f"Vt{i}", [128, S], BF16) for i in range(2)]
        Pb = [A("g3", f"Pb{i}", [128, 4 * BLK], BF16) for i in range(2)]
        PTs = [A("g3", f"PTs{i}", [128, 2 * BLK], BF16) for i in range(2)]
        rrs = A("g3", "rrs", [1, BLK], F32)
        bsum = [A("g3", f"bsum{i}", [128, BLK], F32) for i in range(2)]
        Osb = [A("g3", f"Osb{i}", [128, BLK], BF16) for i in range(2)]

        Og = A("g4", "Og", [128, 32 * BLK], BF16)
        x1sl = [A("g4", f"x1sl{i}", [128, BLK], F32) for i in range(2)]
        wosb = [A("g4", f"wosb{i}", [128, BLK], F32) for i in range(2)]
        x2row = [A("g4", f"x2row{i}", [128, BLK], F32) for i in range(2)]

        pb = [en(nc.psum_tensor(f"pb{i}", [128, BLK], F32)) for i in range(8)]

        def pf32(i):
            return pb[i][:]

        def pbf16(i, n=BLK):
            return pb[i][:].bitcast(BF16)[:, 0:n]



        def dma_s(stream, out, in_, tag=""):
            P.dma("sync", stream,
                  (lambda eng, V, o=out, i=in_: eng.dma_start(out=o, in_=i)),
                  rd=[in_], wr=[out], tag=tag)

        def dma_g(stream, out, in_, tag=""):
            P.dma("gpsimd", stream,
                  (lambda eng, V, o=out, i=in_: eng.dma_start(out=o, in_=i)),
                  rd=[in_], wr=[out], tag=tag)

        dma_s("cst", identb[:], identb_d)
        dma_s("cst", identf[:], identf_d)
        dma_s("cst", cosT[:], cos_d)
        dma_s("cst", sinT[:], sin_d)
        dma_s("cst", dmask[:], dmask_d)
        dma_s("cst", rb1[:], rb)
        dma_g("gw", wrb[:], wr)
        dma_g("gw", ebb[:], eb)
        P.op("vector", lambda eng, V: eng.memset(ones1[:], 1.0), wr=[ones1[:]])
        P.op("tensor",
             lambda eng, V: eng.matmul(pf32(0)[:, 0:E], ones1[:], rb1[:],
                                       start=True, stop=True),
             rd=[ones1[:], rb1[:]], wr=[pf32(0)])
        P.op("vector", lambda eng, V: eng.tensor_copy(rbb[:], pf32(0)[:, 0:E]),
             rd=[pf32(0)], wr=[rbb[:]])
        for dc in range(32):
            bk = 1 + (dc % 2)
            P.op("tensor",
                 lambda eng, V, dc=dc, bk=bk: eng.transpose(
                     pbf16(bk, 128)[0:128, 0:64], wrb[0:64, dc * 128:(dc + 1) * 128],
                     identb[0:64, 0:64]),
                 rd=[wrb[:], identb[:]], wr=[pbf16(bk)])
            P.op("vector",
                 lambda eng, V, dc=dc, bk=bk: eng.tensor_copy(
                     WrT[:, dc * E:(dc + 1) * E], pbf16(bk, 128)[0:128, 0:64]),
                 rd=[pbf16(bk)], wr=[WrT[:]])

        def layer_norm(xin, col, par, want_zf):
            st = lambda k: stat[k][:, col:col + 1]
            P.op("vector", lambda eng, V: eng.reduce_sum(st("mean"), xin[:], AX.X),
                 rd=[xin[:]], wr=[stat["mean"][:]])
            P.op("vector", lambda eng, V: eng.tensor_scalar(
                st("mean"), st("mean"), 1.0 / D, None, ALU.mult),
                rd=[stat["mean"][:]], wr=[stat["mean"][:]])
            P.op("vector", lambda eng, V: eng.tensor_scalar(
                zc[par][:], xin[:], st("mean"), None, ALU.subtract),
                rd=[xin[:], stat["mean"][:]], wr=[zc[par][:]])
            for ch in range(8):
                bk = 6 + (ch % 2)
                P.op("scalar", lambda eng, V, ch=ch, bk=bk: eng.activation(
                    out=pf32(bk), in_=zc[par][:, ch * BLK:(ch + 1) * BLK],
                    func=AF.Square, accum_out=ssq8[:, col * 8 + ch:col * 8 + ch + 1]),
                    rd=[zc[par][:]], wr=[pf32(bk), ssq8[:]])
            P.op("vector", lambda eng, V: eng.reduce_sum(
                st("ssq"), ssq8[:, col * 8:col * 8 + 8], AX.X),
                rd=[ssq8[:]], wr=[stat["ssq"][:]])
            P.op("vector", lambda eng, V: eng.tensor_scalar(
                st("varp"), st("ssq"), 1.0 / D, EPS, ALU.mult, ALU.add),
                rd=[stat["ssq"][:]], wr=[stat["varp"][:]])
            P.op("scalar", lambda eng, V: eng.activation(
                out=st("lnv"), in_=st("varp"), func=AF.Ln),
                rd=[stat["varp"][:]], wr=[stat["lnv"][:]])
            P.op("scalar", lambda eng, V: eng.activation(
                out=st("rstd"), in_=st("lnv"), func=AF.Exp, scale=-0.5),
                rd=[stat["lnv"][:]], wr=[stat["rstd"][:]])
            if want_zf:
                P.op("vector", lambda eng, V: eng.tensor_scalar(
                    zc[par][:], zc[par][:], st("rstd"), None, ALU.mult),
                    rd=[zc[par][:], stat["rstd"][:]], wr=[zc[par][:]])
                P.op("scalar", lambda eng, V: eng.activation(
                    out=zb[:], in_=zc[par][:], func=AF.Copy),
                    rd=[zc[par][:]], wr=[zb[:]])
            else:
                P.op("vector", lambda eng, V: eng.tensor_scalar(
                    zb[:], zc[par][:], st("rstd"), None, ALU.mult),
                    rd=[zc[par][:], stat["rstd"][:]], wr=[zb[:]])

        def ffn(par, xin, xout, zfpar, scale_half=True):
            for dc in range(32):
                bk = (dc % 2)
                P.op("tensor", lambda eng, V, dc=dc, bk=bk: eng.transpose(
                    pbf16(bk, 128), zb[:, dc * 128:(dc + 1) * 128], identb[:]),
                    rd=[zb[:], identb[:]], wr=[pbf16(bk)])
                P.op("vector", lambda eng, V, dc=dc, bk=bk: eng.tensor_copy(
                    zfm[:, dc * 128:(dc + 1) * 128], pbf16(bk, 128)),
                    rd=[pbf16(bk)], wr=[zfm[:]])
            for dc in range(32):
                P.op("tensor", lambda eng, V, dc=dc: eng.matmul(
                    pf32(2)[:, 0:E], zfm[:, dc * 128:(dc + 1) * 128],
                    WrT[:, dc * E:(dc + 1) * E],
                    start=(dc == 0), stop=(dc == 31)),
                    rd=[zfm[:], WrT[:]], wr=[pf32(2)])
            P.op("vector", lambda eng, V: eng.tensor_tensor(
                lgs[par][:], pf32(2)[:, 0:E], rbb[:], ALU.add),
                rd=[pf32(2), rbb[:]], wr=[lgs[par][:]])
            st = lambda k: stat[k][:, par:par + 1]
            P.op("vector", lambda eng, V: eng.reduce_max(st("mx"), lgs[par][:], AX.X),
                 rd=[lgs[par][:]], wr=[stat["mx"][:]])
            P.op("vector", lambda eng, V: eng.tensor_scalar(
                st("nmx"), st("mx"), -1.0, None, ALU.mult),
                rd=[stat["mx"][:]], wr=[stat["nmx"][:]])
            P.op("scalar", lambda eng, V: eng.activation(
                out=exps[par][:], in_=lgs[par][:], func=AF.Exp, bias=st("nmx"),
                accum_out=st("sexp")),
                rd=[lgs[par][:], stat["nmx"][:]], wr=[exps[par][:], stat["sexp"][:]])
            P.op("vector", lambda eng, V: eng.reciprocal(st("rse"), st("sexp")),
                 rd=[stat["sexp"][:]], wr=[stat["rse"][:]])
            P.op("vector", lambda eng, V: eng.tensor_scalar(
                scrs[par][:], exps[par][:], st("rse"), None, ALU.mult),
                rd=[exps[par][:], stat["rse"][:]], wr=[scrs[par][:]])
            P.op("vector", lambda eng, V: eng.max(out=top8[par][:], in_=scrs[par][:]),
                 rd=[scrs[par][:]], wr=[top8[par][:]])
            P.op("vector", lambda eng, V: eng.match_replace(
                out=zap[par][:], in_to_replace=top8[par][:], in_values=scrs[par][:],
                imm_value=0.0),
                rd=[top8[par][:], scrs[par][:]], wr=[zap[par][:]])
            P.op("vector", lambda eng, V: eng.tensor_sub(
                wden[par][:], scrs[par][:], zap[par][:]),
                rd=[scrs[par][:], zap[par][:]], wr=[wden[par][:]])
            P.op("vector", lambda eng, V: eng.reduce_sum(st("wsum"), top8[par][:], AX.X),
                 rd=[top8[par][:]], wr=[stat["wsum"][:]])
            P.op("scalar", lambda eng, V: eng.activation(
                out=wdb[par][:], in_=wden[par][:], func=AF.Copy),
                rd=[wden[par][:]], wr=[wdb[par][:]])
            P.op("tensor", lambda eng, V: eng.transpose(
                pbf16(3, 128)[0:E, 0:128], wdb[par][:, 0:E], identb[:]),
                rd=[wdb[par][:], identb[:]], wr=[pbf16(3)])
            P.op("vector", lambda eng, V: eng.tensor_copy(
                wdT[par][:], pbf16(3, 128)[0:E, 0:128]),
                rd=[pbf16(3)], wr=[wdT[par][:]])
            h = 0.5 if scale_half else 1.0
            for dh in range(8):
                bk = 4 + (dh % 2)
                sl = slice(dh * BLK, (dh + 1) * BLK)
                P.op("tensor", lambda eng, V, dh=dh, bk=bk, sl=sl: eng.matmul(
                    pf32(bk), wdT[par][:], ebb[0:E, sl], start=True, stop=True),
                    rd=[wdT[par][:], ebb[:]], wr=[pf32(bk)])
                P.op("vector", lambda eng, V, dh=dh, bk=bk, sl=sl:
                     eng.scalar_tensor_tensor(
                         xout[:, sl], pf32(bk), h, xin[:, sl], ALU.mult, ALU.add),
                     rd=[pf32(bk), xin[:]], wr=[xout[:]])
                P.op("vector", lambda eng, V, dh=dh, sl=sl: eng.tensor_scalar(
                    zws[par][:], zfpar[:, sl], st("wsum"), h, ALU.mult, ALU.mult),
                    rd=[zfpar[:], stat["wsum"][:]], wr=[zws[par][:]])
                P.op("vector", lambda eng, V, dh=dh, sl=sl: eng.tensor_add(
                    xout[:, sl], xout[:, sl], zws[par][:]),
                    rd=[zws[par][:]], wr=[xout[:]])

        if phase_limit >= 1:
            P.barrier()
            for t in range(8):
                par = t % 2
                tb, ts = t // 4, t % 4
                rows = slice(t * 128, (t + 1) * 128)
                dma_s("xa", xa[par][:], x0[rows, :], tag=f"x0t{t}")
                layer_norm(xa[par], par, par, True)
                ffn(par, xa[par][:], xa[par][:], zc[par])
                dma_s("xw", x1d[rows, :], xa[par][:], tag=f"x1w{t}")
                layer_norm(xa[par], 4 + par, par, False)
                for dc in range(32):
                    bk = (dc % 2)
                    dst = tb * 16384 + dc * BLK + ts * 128
                    P.op("tensor", lambda eng, V, dc=dc, bk=bk: eng.transpose(
                        pbf16(bk, 128), zb[:, dc * 128:(dc + 1) * 128],
                        identb[:]),
                        rd=[zb[:], identb[:]], wr=[pbf16(bk)])
                    P.op("scalar", lambda eng, V, dst=dst, bk=bk: eng.activation(
                        out=z2fm[:, dst:dst + 128], in_=pbf16(bk, 128), func=AF.Copy),
                        rd=[pbf16(bk)], wr=[z2fm[:]])

        if phase_limit >= 2:
            P.barrier()
            mats = [("k", wk, 0), ("q", wq, 2), ("v", wv, 1)]
            def section_ag(sec):
                P.coll(lambda eng, V, sec=sec: eng.collective_compute(
                    "AllGather", ALU.bypass, replica_groups=groups,
                    ins=[kvq_in[sec * D:(sec + 1) * D, :].opt()],
                    outs=[sec_out[sec].opt()],
                ), rd=[f"kvq_in{sec}"], wr=[sec_out[sec]], tag=f"ag{sec}")
                def extract(eng, V, sec=sec):
                    srcap = sec_out[sec].rearrange("(r q) c -> r (q c)", q=D)[
                        :, bass.ds(V["hbK"], NTOK * NTOK)]
                    dst = sec_me[sec].rearrange("(r q) c -> r (q c)", q=NTOK)
                    return eng.dma_start(out=dst, in_=srcap)
                P.dma("gpsimd", "gx", extract, rd=[sec_out[sec]],
                      wr=[f"sec_me{sec}"], tag=f"extract{sec}")
            prev_sec = None
            for mname, wm, sec in mats:
                if prev_sec is not None:
                    section_ag(prev_sec)
                prev_sec = sec
                for hc in range(H):
                    spar = hc % 2
                    strip_src = bass.AP(wm.tensor, hc * 128,
                                        [[D, 128], [128 * D, 32], [1, 128]])
                    P.dma("sync", "ws",
                          (lambda eng, V, s=strip_src, d=wstripf[spar]:
                           eng.dma_start(out=d[:].rearrange(
                               "p (dc c) -> p dc c", c=128), in_=s)),
                          rd=[wm], wr=[wstripf[spar][:]], tag=f"{mname}strip{hc}")
                    P.op("vector", lambda eng, V, spar=spar: eng.tensor_copy(
                        wstrip[spar][:], wstripf[spar][:]),
                        rd=[wstripf[spar][:]], wr=[wstrip[spar][:]])
                    for tb in range(2):
                        pk = tb
                        for dc in range(32):
                            P.op("tensor", lambda eng, V, dc=dc, pk=pk, tb=tb,
                                 spar=spar: eng.matmul(
                                     pf32(pk),
                                     wstrip[spar][:, dc * 128:(dc + 1) * 128],
                                     z2fm[:, tb * 16384 + dc * BLK:
                                          tb * 16384 + (dc + 1) * BLK],
                                     start=(dc == 0), stop=(dc == 31)),
                                 rd=[wstrip[spar][:], z2fm[:]], wr=[pf32(pk)])
                        cpar = (hc * 2 + tb) % 2
                        csl = slice(tb * BLK, (tb + 1) * BLK)
                        if mname in ("q", "k"):
                            P.op("scalar", lambda eng, V, pk=pk, cpar=cpar:
                                 eng.activation(out=qraw[cpar][:], in_=pf32(pk),
                                                func=AF.Copy),
                                 rd=[pf32(pk)], wr=[qraw[cpar][:]])
                            P.op("vector", lambda eng, V, cpar=cpar, csl=csl:
                                 eng.tensor_tensor(rr1[cpar][:], qraw[cpar][:],
                                                   cosT[:, csl], ALU.mult),
                                 rd=[qraw[cpar][:], cosT[:]], wr=[rr1[cpar][:]])
                            P.op("vector", lambda eng, V, cpar=cpar, csl=csl:
                                 eng.tensor_tensor(rr2[cpar][0:64, :],
                                                   qraw[cpar][64:128, :],
                                                   sinT[64:128, csl], ALU.mult),
                                 rd=[qraw[cpar][:], sinT[:]], wr=[rr2[cpar][:]])
                            P.op("vector", lambda eng, V, cpar=cpar, csl=csl:
                                 eng.tensor_tensor(rr2[cpar][64:128, :],
                                                   qraw[cpar][0:64, :],
                                                   sinT[0:64, csl], ALU.mult),
                                 rd=[qraw[cpar][:], sinT[:]], wr=[rr2[cpar][:]])
                            P.op("vector", lambda eng, V, cpar=cpar:
                                 eng.tensor_add(rob[cpar][:], rr1[cpar][:],
                                                rr2[cpar][:]),
                                 rd=[rr1[cpar][:], rr2[cpar][:]], wr=[rob[cpar][:]])
                            src_t = rob[cpar]
                        else:
                            P.op("scalar", lambda eng, V, pk=pk, cpar=cpar:
                                 eng.activation(out=vtmp[cpar][:], in_=pf32(pk),
                                                func=AF.Copy),
                                 rd=[pf32(pk)], wr=[vtmp[cpar][:]])
                            src_t = vtmp[cpar]
                        dst = kvq_in[sec * D + hc * 128: sec * D + (hc + 1) * 128,
                                     tb * BLK:(tb + 1) * BLK]
                        kkey = f"kvq_in{sec}"
                        P.dma("sync", "xw",
                              (lambda eng, V, o=dst, s=src_t: eng.dma_start(
                                  out=o, in_=s[:])),
                              rd=[src_t[:]], wr=[kkey], tag=f"{mname}w{hc}_{tb}")

            section_ag(prev_sec)

        if phase_limit >= 3:
            P.barrier()

            def gather_load(sec, i, dst):
                src = bass.AP(sec_me[sec].tensor, (i * 128) * NTOK,
                              [[NTOK, 128], [NTOK * NTOK, 4], [1, NTOK]])
                P.dma("sync", "kv",
                      (lambda eng, V, s=src, d=dst: eng.dma_start(
                          out=d[:].rearrange("p (r c) -> p r c", c=NTOK), in_=s)),
                      rd=[f"sec_me{sec}"], wr=[dst[:]], tag=f"g{sec}_{i}")

            for i in range(HL):
                par = i % 2
                gather_load(0, i, Kh[par])
                gather_load(1, i, Vf[par])
                gather_load(2, i, Qh[par])
                for kc in range(32):
                    bk = kc % 2
                    P.op("tensor", lambda eng, V, kc=kc, bk=bk, par=par:
                         eng.transpose(pbf16(bk, 128),
                                       Vf[par][:, kc * 128:(kc + 1) * 128],
                                       identb[:]),
                         rd=[Vf[par][:], identb[:]], wr=[pbf16(bk)])
                    P.op("vector", lambda eng, V, kc=kc, bk=bk, par=par:
                         eng.tensor_copy(Vt[par][:, kc * 128:(kc + 1) * 128],
                                         pbf16(bk, 128)),
                         rd=[pbf16(bk)], wr=[Vt[par][:]])
                def emit_s_exp(gq, kb, qcol):
                    kcol = col_of(kb)
                    ppar = kb % 2
                    spar = kb % 2
                    for qs in range(4):
                        sbk = 2 + (qs % 2)
                        P.op("tensor", lambda eng, V, qs=qs, sbk=sbk, par=par,
                             qcol=qcol, kcol=kcol: eng.matmul(
                                 pf32(sbk),
                                 Qh[par][:, qcol + qs * 128: qcol + qs * 128 + 128],
                                 Kh[par][:, kcol:kcol + BLK],
                                 start=True, stop=True),
                             rd=[Qh[par][:], Kh[par][:]], wr=[pf32(sbk)])
                        if kb == gq:
                            P.op("scalar", lambda eng, V, qs=qs, sbk=sbk,
                                 ppar=ppar: eng.activation(
                                     out=Pb[ppar][:, qs * BLK:(qs + 1) * BLK],
                                     in_=pf32(sbk), func=AF.Exp, scale=ISQ),
                                 rd=[pf32(sbk)], wr=[Pb[ppar][:]])
                        else:
                            P.op("scalar", lambda eng, V, qs=qs, sbk=sbk, spar=spar,
                                 ppar=ppar: eng.activation(
                                     out=Pb[ppar][:, qs * BLK:(qs + 1) * BLK],
                                     in_=pf32(sbk), func=AF.Exp, scale=ISQ,
                                     accum_out=sblk[:, spar * 4 + qs:
                                                    spar * 4 + qs + 1]),
                                 rd=[pf32(sbk)],
                                 wr=[Pb[ppar][:], f"sblk{spar}"], tag="exp")

                def emit_sums(gq, kb):
                    ppar = kb % 2
                    spar = kb % 2
                    if kb == gq:
                        P.op("vector", lambda eng, V, ppar=ppar:
                             eng.tensor_mul(Pb[ppar][:], Pb[ppar][:], dmask[:]),
                             rd=[dmask[:], Pb[ppar][:]], wr=[Pb[ppar][:]])
                        P.op("vector", lambda eng, V, ppar=ppar, spar=spar:
                             eng.tensor_reduce(
                                 sblk[:, spar * 4:spar * 4 + 4].rearrange(
                                     "p (a b) -> p a b", b=1),
                                 Pb[ppar][:].rearrange("p (a c) -> p a c", c=BLK),
                                 AX.X, ALU.add),
                             rd=[Pb[ppar][:]], wr=[f"sblk{spar}"])
                    P.op("vector", lambda eng, V, spar=spar: eng.tensor_add(
                        sums[:], sums[:], sblk[:, spar * 4:spar * 4 + 4]),
                        rd=[f"sblk{spar}"], wr=[sums[:]])

                def emit_tp(gq, kb, obk, nkc0):
                    kcol = col_of(kb)
                    ppar = kb % 2
                    for kc in range(4):
                        tbk = 6 + (kc // 2)
                        for qs in range(4):
                            P.op("tensor", lambda eng, V, qs=qs, kc=kc, tbk=tbk,
                                 ppar=ppar: eng.transpose(
                                     pbf16(tbk, 1024)[:, (kc % 2) * BLK + qs * 128:
                                                      (kc % 2) * BLK + (qs + 1) * 128],
                                     Pb[ppar][:, qs * BLK + kc * 128:
                                              qs * BLK + (kc + 1) * 128],
                                     identb[:]),
                                 rd=[Pb[ppar][:], identb[:]], wr=[pbf16(tbk)])
                    for pr in range(2):
                        P.op("vector", lambda eng, V, pr=pr:
                             eng.tensor_copy(PTs[pr][:], pbf16(6 + pr, 1024)),
                             rd=[pbf16(6 + pr)], wr=[PTs[pr][:]])
                    for kc in range(4):
                        P.op("tensor", lambda eng, V, kc=kc, kcol=kcol, obk=obk,
                             par=par, first=(nkc0 + kc == 0),
                             last=(kb == gq and kc == 3): eng.matmul(
                                 pf32(obk),
                                 Vt[par][:, kcol + kc * 128:kcol + (kc + 1) * 128],
                                 PTs[kc // 2][:, (kc % 2) * BLK:(kc % 2 + 1) * BLK],
                                 start=first, stop=last),
                             rd=[Vt[par][:], PTs[kc // 2][:]], wr=[pf32(obk)])

                for gq in range(NB):
                    qcol = col_of(gq)
                    obk = 4 + (gq % 2)
                    P.op("vector", lambda eng, V: eng.memset(sums[:], 0.0),
                         wr=[sums[:]])
                    for kb in range(gq + 1):
                        emit_s_exp(gq, kb, qcol)
                        if kb > 0:
                            emit_tp(gq, kb - 1, obk, (kb - 1) * 4)
                        emit_sums(gq, kb)
                    emit_tp(gq, gq, obk, gq * 4)
                    P.op("vector", lambda eng, V: eng.reciprocal(rs4[:], sums[:]),
                         rd=[sums[:]], wr=[rs4[:]])
                    for qs in range(4):
                        P.op("tensor", lambda eng, V, qs=qs: eng.transpose(
                            pf32(3)[0:1, qs * 128:(qs + 1) * 128],
                            rs4[:, qs:qs + 1], identf[:]),
                            rd=[rs4[:], identf[:]], wr=[pf32(3)])
                    P.op("vector", lambda eng, V: eng.tensor_copy(
                        rrs[:], pf32(3)[0:1, 0:BLK]),
                        rd=[pf32(3)], wr=[rrs[:]])
                    P.op("tensor", lambda eng, V: eng.matmul(
                        pf32(3), ones1[:], rrs[:], start=True, stop=True),
                        rd=[ones1[:], rrs[:]], wr=[pf32(3)])
                    bpar = gq % 2
                    P.op("vector", lambda eng, V, bpar=bpar: eng.tensor_copy(
                        bsum[bpar][:], pf32(3)),
                        rd=[pf32(3)], wr=[bsum[bpar][:]])
                    P.op("vector", lambda eng, V, obk=obk, bpar=bpar:
                         eng.tensor_tensor(Osb[bpar][:], pf32(obk), bsum[bpar][:],
                                           ALU.mult),
                         rd=[pf32(obk), bsum[bpar][:]], wr=[Osb[bpar][:]])
                    dma_s("xw", o_in[i * 128:(i + 1) * 128, gq * BLK:(gq + 1) * BLK],
                          Osb[bpar][:], tag=f"ow{i}_{gq}")

            P.coll(lambda eng, V: eng.collective_compute(
                "AllGather", ALU.bypass, replica_groups=groups,
                ins=[o_in.opt()], outs=[o_out.opt()]),
                rd=[o_in], wr=[o_out], tag="ag_o")

        if phase_limit >= 4:
            P.barrier()
            for tb in range(2):
                def og_load(eng, V, tb=tb):
                    col = V["mycol0"] if tb == 0 else V["mycol1"]
                    src = o_out.rearrange("(h p) c -> p h c", p=128)[
                        :, :, bass.ds(col, BLK)]
                    return eng.dma_start(
                        out=Og[:].rearrange("p (h c) -> p h c", c=BLK), in_=src)
                P.dma("sync", "kv", og_load, rd=[o_out], wr=[Og[:]], tag=f"og{tb}")
                for dc in range(32):
                    spar = dc % 2
                    strip_src = bass.AP(wo.tensor, dc * 128,
                                        [[D, 128], [128 * D, 32], [1, 128]])
                    P.dma("sync", "ws",
                          (lambda eng, V, s=strip_src, d=wstripf[spar]:
                           eng.dma_start(out=d[:].rearrange(
                               "p (a c) -> p a c", c=128), in_=s)),
                          rd=[wo], wr=[wstripf[spar][:]], tag=f"wostrip{tb}_{dc}")
                    P.op("vector", lambda eng, V, spar=spar: eng.tensor_copy(
                        wstrip[spar][:], wstripf[spar][:]),
                        rd=[wstripf[spar][:]], wr=[wstrip[spar][:]])
                    x1src = bass.AP(x1d.tensor, tb * BLK * D + dc * 128,
                                    [[D, 128], [128 * D, 4], [1, 128]])
                    xspar = dc % 2
                    P.dma("sync", "xa",
                          (lambda eng, V, s=x1src, d=x1sl[xspar]:
                           eng.dma_start(out=d[:].rearrange(
                               "p (t c) -> p t c", c=128), in_=s)),
                          rd=[x1d], wr=[x1sl[xspar][:]], tag=f"x1r{tb}_{dc}")
                    pk = dc % 2
                    for ad in range(32):
                        P.op("tensor", lambda eng, V, ad=ad, pk=pk, spar=spar:
                             eng.matmul(
                                 pf32(pk),
                                 wstrip[spar][:, ad * 128:(ad + 1) * 128],
                                 Og[:, ad * BLK:(ad + 1) * BLK],
                                 start=(ad == 0), stop=(ad == 31)),
                             rd=[wstrip[spar][:], Og[:]], wr=[pf32(pk)])
                    wpar = dc % 2
                    P.op("scalar", lambda eng, V, pk=pk, wpar=wpar: eng.activation(
                        out=wosb[wpar][:], in_=pf32(pk), func=AF.Copy),
                        rd=[pf32(pk)], wr=[wosb[wpar][:]])
                    xrpar = dc % 2
                    for ts in range(4):
                        tbk = 2 + (ts % 2)
                        P.op("tensor", lambda eng, V, ts=ts, tbk=tbk, wpar=wpar:
                             eng.transpose(pf32(tbk)[:, 0:128],
                                           wosb[wpar][:, ts * 128:(ts + 1) * 128],
                                           identf[:]),
                             rd=[wosb[wpar][:], identf[:]], wr=[pf32(tbk)])
                        P.op("vector", lambda eng, V, ts=ts, tbk=tbk, xspar=xspar,
                             xrpar=xrpar: eng.scalar_tensor_tensor(
                                 x2row[xrpar][:, ts * 128:(ts + 1) * 128],
                                 pf32(tbk)[:, 0:128], 1.0,
                                 x1sl[xspar][:, ts * 128:(ts + 1) * 128],
                                 ALU.mult, ALU.add),
                             rd=[pf32(tbk), x1sl[xspar][:]], wr=[x2row[xrpar][:]])
                    x2dst = bass.AP(x2d.tensor, (tb * BLK * D) + dc * 128,
                                    [[D, 128], [128 * D, 4], [1, 128]])
                    P.dma("sync", "xw",
                          (lambda eng, V, o=x2dst, s=x2row[xrpar]: eng.dma_start(
                              out=o, in_=s[:].rearrange("p (t c) -> p t c", c=128))),
                          rd=[x2row[xrpar][:]], wr=[x2d], tag=f"x2w{tb}_{dc}")

        if phase_limit >= 5:
            P.barrier()
            for t in range(8):
                par = t % 2
                rows = slice(t * 128, (t + 1) * 128)
                dma_s("xa", xa[par][:], x2d[rows, :], tag=f"x2t{t}")
                layer_norm(xa[par], par, par, True)
                ffn(par, xa[par][:], xa[par][:], zc[par])
                dma_s("outw", out_d[rows, :], xa[par][:], tag=f"outw{t}")

        P.barrier(tag="final")

        semset = set(P.sems) | set(P.cnt.keys())
        sems = {}
        for name in sorted(semset):
            sems[name] = en(nc.semaphore(name))

        def val_hook(eng):
            pid = eng.partition_id()
            rank = pid % 4
            hbK = eng.compute_val(rank * (NTOK * NTOK))
            mycol0 = eng.compute_val(rank * BLK)
            mycol1 = eng.compute_val((7 * BLK) - rank * BLK)
            return {"hbK": hbK, "mycol0": mycol0, "mycol1": mycol1}

        P.emit_all(nc, sems, val_hook)

    return nc


def make_consts():
    identb = np.eye(128, dtype=ml_dtypes.bfloat16)
    identf = np.eye(128, dtype=np.float32)
    dmask = np.zeros((128, 4 * BLK), dtype=ml_dtypes.bfloat16)
    for qs in range(4):
        qi = np.arange(128)[:, None]
        kj = np.arange(BLK)[None, :]
        dmask[:, qs * BLK:(qs + 1) * BLK] = (kj <= qs * 128 + qi).astype(
            ml_dtypes.bfloat16)
    return identb, identf, dmask


def rope_tables(positions):
    inv = (1.0 / 10000.0 ** (np.arange(0, HD, 2, dtype=np.float32) / HD)).astype(
        np.float32)
    freqs = positions[None, :].astype(np.float32) * inv[:, None]
    cos = np.cos(freqs).astype(np.float32)
    sin = np.sin(freqs).astype(np.float32)
    cosT = np.concatenate([cos, cos], axis=0)
    sinT = np.concatenate([sin, -sin], axis=0)
    return cosT, sinT


_NC_CACHE = {}


def kernel(L0_attn_hidden_states, L0_attn_wq, L0_attn_wk, L0_attn_wv, L0_attn_wo,
           L0_attn_kq1, L0_attn_kq2, L0_attn_kk1, L0_attn_kk2, L0_attn_scalingD,
           L0_attn_interaction_k, L0_attn_scalingD2, L0_ffn_router_weight,
           L0_ffn_router_bias, L0_ffn_expert_bias):
    x = np.ascontiguousarray(np.asarray(L0_attn_hidden_states, dtype=np.float32))
    wq = np.ascontiguousarray(np.asarray(L0_attn_wq, dtype=np.float32))
    wk = np.ascontiguousarray(np.asarray(L0_attn_wk, dtype=np.float32))
    wv = np.ascontiguousarray(np.asarray(L0_attn_wv, dtype=np.float32))
    wo = np.ascontiguousarray(np.asarray(L0_attn_wo, dtype=np.float32))
    wr = np.ascontiguousarray(np.asarray(L0_ffn_router_weight, dtype=np.float32))
    rb = np.ascontiguousarray(
        np.asarray(L0_ffn_router_bias, dtype=np.float32)).reshape(1, E)
    eb = np.ascontiguousarray(np.asarray(L0_ffn_expert_bias, dtype=np.float32))

    identb, identf, dmask = make_consts()

    key = "nc8"
    if key not in _NC_CACHE:
        _NC_CACHE[key] = build_nc(n_cores=8)
    nc = _NC_CACHE[key]

    in_maps = []
    for c in range(8):
        b, e = c // 4, c % 4
        rows_a = np.arange(e * BLK, (e + 1) * BLK)
        rows_b = np.arange((7 - e) * BLK, (8 - e) * BLK)
        rows = np.concatenate([rows_a, rows_b])
        x0c = np.ascontiguousarray(x[b][rows])
        cosT, sinT = rope_tables(rows)
        in_maps.append({
            "x0": x0c, "wq": wq, "wk": wk, "wv": wv, "wo": wo,
            "wr": wr, "rb": rb, "eb": eb,
            "identb": identb, "identf": identf,
            "cosT": np.ascontiguousarray(cosT),
            "sinT": np.ascontiguousarray(sinT),
            "dmask": dmask,
        })

    res = run_bass_kernel_spmd(nc, in_maps, core_ids=list(range(8)))

    out = np.empty((2, S, D), dtype=np.float32)
    for c in range(8):
        b, e = c // 4, c % 4
        oc = res.results[c]["out"]
        out[b, e * BLK:(e + 1) * BLK] = oc[:BLK]
        out[b, (7 - e) * BLK:(8 - e) * BLK] = oc[BLK:]
    return out


# version 09 (final)
# speedup vs baseline: 1.3569x; 1.3569x over previous
import math
from contextlib import ExitStack

import numpy as np
import ml_dtypes

import concourse.bass as bass
import concourse.mybir as mybir
from concourse.bass_utils import run_bass_kernel_spmd

F32 = mybir.dt.float32
BF16 = mybir.dt.bfloat16
AF = mybir.ActivationFunctionType
ALU = mybir.AluOpType
AX = mybir.AxisListType

S = 4096
D = 4096
H = 32
HD = 128
E = 64
NTOK = 1024
NB = 8
BLK = 512
HL = 8
EPS = 1e-5
ISQ = 1.0 / math.sqrt(HD)


def owner(j):
    return j if j < 4 else 7 - j


def half(j):
    return 0 if j < 4 else 1


def col_of(j):
    return owner(j) * 1024 + half(j) * 512


class Prog:
    ENG = ("tensor", "vector", "scalar", "gpsimd", "sync")

    def __init__(self):
        self.ops = {e: [] for e in self.ENG}
        self.cnt = {}
        self.clock = {e: {} for e in self.ENG}
        self.snaps = {}
        self.wr = {}
        self.rd = {}
        self.stream_q = {}
        self.sems = set(self.ENG)

    @staticmethod
    def _key(x):
        if isinstance(x, str):
            return x, False
        return x.name, str(x.space).upper().endswith("PSUM")

    def _need(self, eng, marker):
        sem, val = marker
        if sem == "tensor" and eng == "tensor":
            return False
        return self.clock[eng].get(sem, 0) < val

    def _observe(self, eng, marker):
        sem, val = marker
        c = self.clock[eng]
        c[sem] = max(c.get(sem, 0), val)
        for s, v in self.snaps.get(marker, {}).items():
            c[s] = max(c.get(s, 0), v)

    def _waits_for(self, eng, rds, wrs):
        needs = {}

        def add(m):
            if m is not None and self._need(eng, m):
                s, v = m
                needs[s] = max(needs.get(s, 0), v)

        for k in rds:
            add(self.wr.get(k))
        for k in wrs:
            add(self.wr.get(k))
            for m in self.rd.get(k, ()):
                add(m)
        waits = sorted(needs.items())
        for m in waits:
            self._observe(eng, m)
        return waits

    def _commit(self, marker, rds, wrs):
        for k in rds:
            self.rd.setdefault(k, []).append(marker)
        for k in wrs:
            self.wr[k] = marker
            self.rd[k] = []

    def _split(self, rd, wr):
        rds, wrs = [], []
        for x in rd:
            k, is_psum = self._key(x)
            (wrs if is_psum else rds).append(k)
        for x in wr:
            k, _ = self._key(x)
            wrs.append(k)
        return rds, wrs

    def op(self, eng, emit, rd=(), wr=(), tag=""):
        rds, wrs = self._split(rd, wr)
        waits = self._waits_for(eng, rds, wrs)
        self.cnt[eng] = self.cnt.get(eng, 0) + 1
        marker = (eng, self.cnt[eng])
        self.snaps[marker] = dict(self.clock[eng]) | {eng: self.cnt[eng]}
        self._commit(marker, rds, wrs)
        self.ops[eng].append((waits, emit, eng, 1, tag))
        return marker

    def dma(self, queue, stream, emit, rd=(), wr=(), tag=""):
        assert self.stream_q.setdefault(stream, queue) == queue, (
            f"stream {stream} used from two queues")
        R = 2 if queue == "gpsimd" else 4
        k = self.rr = getattr(self, "rr", {})
        n0 = k.get(stream, 0)
        k[stream] = n0 + 1
        stream = f"{stream}{n0 % R}"
        self.sems.add(stream)
        rds, wrs = self._split(rd, wr)
        prev_val = self.cnt.get(stream, 0)
        needs = {}
        if prev_val and self.clock[queue].get(stream, 0) < prev_val:
            needs[stream] = prev_val
        waits = self._waits_for(queue, rds, wrs)
        for s, v in waits:
            needs[s] = max(needs.get(s, 0), v)
        waits = sorted(needs.items())
        for m in waits:
            self._observe(queue, m)
        n = prev_val + 16
        self.cnt[stream] = n
        marker = (stream, n)
        prev = self.snaps.get((stream, n - 16), {})
        snap = dict(prev)
        for s, v in (self.clock[queue] | {stream: n}).items():
            snap[s] = max(snap.get(s, 0), v)
        self.snaps[marker] = snap
        self._commit(marker, rds, wrs)
        self.ops[queue].append((waits, emit, stream, 16, tag))
        return marker

    def coll(self, emit, rd=(), wr=(), tag="coll"):
        self.sems.add("coll")
        rds, wrs = self._split(rd, wr)
        waits = self._waits_for("gpsimd", rds, wrs)
        n = self.cnt.get("coll", 0) + 1
        self.cnt["coll"] = n
        marker = ("coll", n)
        snap = dict(self.snaps.get(("coll", n - 1), {}))
        for s, v in (self.clock["gpsimd"] | {"coll": n}).items():
            snap[s] = max(snap.get(s, 0), v)
        self.snaps[marker] = snap
        self._commit(marker, rds, wrs)
        self.ops["gpsimd"].append((waits, emit, "coll", 1, tag))
        return marker

    def barrier(self, tag="barrier"):
        targets = [(s, v) for s, v in self.cnt.items()
                   if not s.startswith(("wcast", "coll", "gx"))]
        for eng in self.ENG:
            waits = []
            for m in targets:
                if m[0] != eng and self._need(eng, m):
                    waits.append(m)
                    self._observe(eng, m)
            if waits:
                self.ops[eng].append((sorted(waits), None, eng, 0, tag))

    def emit_all(self, nc, sems, val_hook=None):
        def body_for(en):
            def body(eng):
                V = {}
                if en in ("sync", "gpsimd") and val_hook is not None:
                    V.update(val_hook(eng))
                for waits, emit, inc_sem, inc_val, tag in self.ops[en]:
                    for s, v in waits:
                        eng.wait_ge(sems[s], v)
                    if emit is not None:
                        inst = emit(eng, V)
                        if inc_val:
                            inst.then_inc(sems[inc_sem], inc_val)
            return body

        with nc.Block() as block:
            for en in self.ENG:
                if self.ops[en]:
                    getattr(block, en)(body_for(en))


def build_nc(n_cores=8, phase_limit=99):
    nc = bass.Bass("TRN2", target_bir_lowering=False)
    P = Prog()

    groups = [[0, 1, 2, 3], [4, 5, 6, 7]] if n_cores == 8 else [[0, 1, 2, 3]]

    x0 = nc.dram_tensor("x0", [NTOK, D], F32, kind="ExternalInput").ap()
    wq = nc.dram_tensor("wq", [D, D], F32, kind="ExternalInput").ap()
    wk = nc.dram_tensor("wk", [D, D], F32, kind="ExternalInput").ap()
    wv = nc.dram_tensor("wv", [D, D], F32, kind="ExternalInput").ap()
    wo = nc.dram_tensor("wo", [D, D], F32, kind="ExternalInput").ap()
    wr = nc.dram_tensor("wr", [E, D], F32, kind="ExternalInput").ap()
    rb = nc.dram_tensor("rb", [1, E], F32, kind="ExternalInput").ap()
    eb = nc.dram_tensor("eb", [E, D], F32, kind="ExternalInput").ap()
    identb_d = nc.dram_tensor("identb", [128, 128], BF16, kind="ExternalInput").ap()
    identf_d = nc.dram_tensor("identf", [128, 128], F32, kind="ExternalInput").ap()
    cos_d = nc.dram_tensor("cosT", [128, NTOK], F32, kind="ExternalInput").ap()
    sin_d = nc.dram_tensor("sinT", [128, NTOK], F32, kind="ExternalInput").ap()
    dmask_d = nc.dram_tensor("dmask", [128, 4 * BLK], BF16, kind="ExternalInput").ap()
    out_d = nc.dram_tensor("out", [NTOK, D], F32, kind="ExternalOutput").ap()

    x1d = nc.dram_tensor("x1d", [NTOK, D], F32).ap()
    x2d = nc.dram_tensor("x2d", [NTOK, D], F32).ap()
    kvq_in = nc.dram_tensor("kvq_in", [3 * D, NTOK], BF16).ap()
    sec_out = [nc.dram_tensor(f"sec_out{s}", [4 * D, NTOK], BF16).ap()
               for s in range(3)]
    sec_me = [nc.dram_tensor(f"sec_me{s}", [D, NTOK], BF16).ap()
              for s in range(3)]
    o_in = nc.dram_tensor("o_in", [HL * HD, S], BF16).ap()
    o_out = nc.dram_tensor("o_out", [4 * HL * HD, S], BF16).ap()

    with ExitStack() as ctx:
        en = ctx.enter_context

        sb = {}

        def T(name, shape, dtype):
            sb[name] = en(nc.sbuf_tensor(name, shape, dtype))
            return sb[name]

        identb = T("s_identb", [128, 128], BF16)
        identf = T("s_identf", [128, 128], F32)
        cosT = T("s_cosT", [128, NTOK], F32)
        sinT = T("s_sinT", [128, NTOK], F32)
        dmask = T("s_dmask", [128, 4 * BLK], BF16)
        ones1 = T("ones1", [1, 128], F32)
        WrT = T("WrT", [128, 32 * E], BF16)
        rb1 = T("rb1", [1, E], F32)
        rbb = T("rbb", [128, E], F32)
        ebb = T("ebb", [E, D], BF16)
        stat = {k: T(k, [128, 8], F32) for k in
                ("mean", "ssq", "varp", "lnv", "rstd", "mx", "nmx", "sexp", "rse",
                 "wsum")}
        ssq8 = T("ssq8", [128, 64], F32)
        lgs = [T(f"lgs{i}", [128, E], F32) for i in range(2)]
        exps = [T(f"exps{i}", [128, E], F32) for i in range(2)]
        scrs = [T(f"scrs{i}", [128, E], F32) for i in range(2)]
        top8 = [T(f"top8{i}", [128, 8], F32) for i in range(2)]
        zap = [T(f"zap{i}", [128, E], F32) for i in range(2)]
        wden = [T(f"wden{i}", [128, E], F32) for i in range(2)]
        wdb = [T(f"wdb{i}", [128, E], BF16) for i in range(2)]
        wdT = [T(f"wdT{i}", [E, 128], BF16) for i in range(2)]
        sums = T("sums", [128, 4], F32)
        sblk = T("sblk", [128, 8], F32)
        rs4 = T("rs4", [128, 4], F32)
        z2fm = T("z2fm", [128, 2 * 32 * BLK], BF16)

        ARENA = 88 * 1024
        arena = en(nc.sbuf_tensor("arena", [128, ARENA], mybir.dt.uint8))
        arena_addr = nc.lookup_mloc(arena).addr
        _goff = {}

        def A(group, name, shape, dtype):
            per = 1
            for s in shape[1:]:
                per *= s
            per *= mybir.dt.size(dtype)
            off = (_goff.get(group, 0) + 31) // 32 * 32
            _goff[group] = off + per
            assert _goff[group] <= ARENA, (group, name, _goff[group])
            h = nc.virtual_sbuf_tensor(name, list(shape), dtype)
            nc._tensor(h.name, h.shape, h.dtype, type="SB")
            mloc = nc.lookup_mloc(h)
            mloc.allocated = True
            mloc.addr = arena_addr + off
            sb[name] = h
            return h

        wrb = A("g0", "wrb", [E, D], BF16)

        wstrip = [A("g2", f"wstrip{i}", [128, 2 * D], BF16) for i in range(2)]
        _goff["g4"] = _goff["g2"]

        xa = [A("g1", f"xa{i}", [128, D], F32) for i in range(2)]
        zc = [A("g1", f"zc{i}", [128, D], F32) for i in range(2)]
        zb = A("g1", "zb", [128, D], BF16)
        zfm = A("g1", "zfm", [128, D], BF16)
        zws = [A("g1", f"zws{i}", [128, BLK], F32) for i in range(2)]

        qraw = [A("g2", f"qraw{i}", [128, BLK], F32) for i in range(2)]
        rr1 = [A("g2", f"rr1{i}", [128, BLK], F32) for i in range(2)]
        rr2 = [A("g2", f"rr2{i}", [128, BLK], F32) for i in range(2)]
        rob = [A("g2", f"rob{i}", [128, BLK], BF16) for i in range(2)]
        vtmp = [A("g2", f"vtmp{i}", [128, BLK], BF16) for i in range(2)]

        Kh = [A("g3", f"Kh{i}", [128, S], BF16) for i in range(2)]
        Vf = [A("g3", f"Vf{i}", [128, S], BF16) for i in range(2)]
        Qh = [A("g3", f"Qh{i}", [128, S], BF16) for i in range(2)]
        Vt = [A("g3", f"Vt{i}", [128, S], BF16) for i in range(2)]
        Pb = [A("g3", f"Pb{i}", [128, 4 * BLK], BF16) for i in range(2)]
        PTs = [A("g3", f"PTs{i}", [128, 2 * BLK], BF16) for i in range(2)]
        rrs = A("g3", "rrs", [1, BLK], F32)
        bsum = [A("g3", f"bsum{i}", [128, BLK], F32) for i in range(2)]
        Osb = [A("g3", f"Osb{i}", [128, BLK], BF16) for i in range(2)]

        Og = A("g4", "Og", [128, 32 * BLK], BF16)
        x1sl = [A("g4", f"x1sl{i}", [128, BLK], F32) for i in range(2)]
        wosb = [A("g4", f"wosb{i}", [128, BLK], F32) for i in range(2)]
        x2row = [A("g4", f"x2row{i}", [128, BLK], F32) for i in range(2)]

        pb = [en(nc.psum_tensor(f"pb{i}", [128, BLK], F32)) for i in range(8)]

        def pf32(i):
            return pb[i][:]

        def pbf16(i, n=BLK):
            return pb[i][:].bitcast(BF16)[:, 0:n]



        def dma_s(stream, out, in_, tag=""):
            P.dma("sync", stream,
                  (lambda eng, V, o=out, i=in_: eng.dma_start(out=o, in_=i)),
                  rd=[in_], wr=[out], tag=tag)

        def dma_g(stream, out, in_, tag=""):
            P.dma("gpsimd", stream,
                  (lambda eng, V, o=out, i=in_: eng.dma_start(out=o, in_=i)),
                  rd=[in_], wr=[out], tag=tag)

        dma_s("cst", identb[:], identb_d)
        dma_s("cst", identf[:], identf_d)
        dma_s("cst", cosT[:], cos_d)
        dma_s("cst", sinT[:], sin_d)
        dma_s("cst", dmask[:], dmask_d)
        dma_s("cst", rb1[:], rb)
        dma_g("gw", wrb[:], wr)
        dma_g("gw", ebb[:], eb)
        P.op("vector", lambda eng, V: eng.memset(ones1[:], 1.0), wr=[ones1[:]])
        P.op("tensor",
             lambda eng, V: eng.matmul(pf32(0)[:, 0:E], ones1[:], rb1[:],
                                       start=True, stop=True),
             rd=[ones1[:], rb1[:]], wr=[pf32(0)])
        P.op("vector", lambda eng, V: eng.tensor_copy(rbb[:], pf32(0)[:, 0:E]),
             rd=[pf32(0)], wr=[rbb[:]])
        for dc in range(32):
            bk = 1 + (dc % 2)
            P.op("tensor",
                 lambda eng, V, dc=dc, bk=bk: eng.transpose(
                     pbf16(bk, 128)[0:128, 0:64], wrb[0:64, dc * 128:(dc + 1) * 128],
                     identb[0:64, 0:64]),
                 rd=[wrb[:], identb[:]], wr=[pbf16(bk)])
            P.op("vector",
                 lambda eng, V, dc=dc, bk=bk: eng.tensor_copy(
                     WrT[:, dc * E:(dc + 1) * E], pbf16(bk, 128)[0:128, 0:64]),
                 rd=[pbf16(bk)], wr=[WrT[:]])

        def layer_norm(xin, col, par, want_zf):
            st = lambda k: stat[k][:, col:col + 1]
            P.op("vector", lambda eng, V: eng.reduce_sum(st("mean"), xin[:], AX.X),
                 rd=[xin[:]], wr=[stat["mean"][:]])
            P.op("vector", lambda eng, V: eng.tensor_scalar(
                st("mean"), st("mean"), 1.0 / D, None, ALU.mult),
                rd=[stat["mean"][:]], wr=[stat["mean"][:]])
            P.op("vector", lambda eng, V: eng.tensor_scalar(
                zc[par][:], xin[:], st("mean"), None, ALU.subtract),
                rd=[xin[:], stat["mean"][:]], wr=[zc[par][:]])
            for ch in range(8):
                bk = 6 + (ch % 2)
                P.op("scalar", lambda eng, V, ch=ch, bk=bk: eng.activation(
                    out=pf32(bk), in_=zc[par][:, ch * BLK:(ch + 1) * BLK],
                    func=AF.Square, accum_out=ssq8[:, col * 8 + ch:col * 8 + ch + 1]),
                    rd=[zc[par][:]], wr=[pf32(bk), ssq8[:]])
            P.op("vector", lambda eng, V: eng.reduce_sum(
                st("ssq"), ssq8[:, col * 8:col * 8 + 8], AX.X),
                rd=[ssq8[:]], wr=[stat["ssq"][:]])
            P.op("vector", lambda eng, V: eng.tensor_scalar(
                st("varp"), st("ssq"), 1.0 / D, EPS, ALU.mult, ALU.add),
                rd=[stat["ssq"][:]], wr=[stat["varp"][:]])
            P.op("scalar", lambda eng, V: eng.activation(
                out=st("lnv"), in_=st("varp"), func=AF.Ln),
                rd=[stat["varp"][:]], wr=[stat["lnv"][:]])
            P.op("scalar", lambda eng, V: eng.activation(
                out=st("rstd"), in_=st("lnv"), func=AF.Exp, scale=-0.5),
                rd=[stat["lnv"][:]], wr=[stat["rstd"][:]])
            if want_zf:
                P.op("vector", lambda eng, V: eng.tensor_scalar(
                    zc[par][:], zc[par][:], st("rstd"), None, ALU.mult),
                    rd=[zc[par][:], stat["rstd"][:]], wr=[zc[par][:]])
                P.op("scalar", lambda eng, V: eng.activation(
                    out=zb[:], in_=zc[par][:], func=AF.Copy),
                    rd=[zc[par][:]], wr=[zb[:]])
            else:
                P.op("vector", lambda eng, V: eng.tensor_scalar(
                    zb[:], zc[par][:], st("rstd"), None, ALU.mult),
                    rd=[zc[par][:], stat["rstd"][:]], wr=[zb[:]])

        def ffn(par, xin, xout, zfpar, scale_half=True):
            for dc in range(32):
                bk = (dc % 2)
                P.op("tensor", lambda eng, V, dc=dc, bk=bk: eng.transpose(
                    pbf16(bk, 128), zb[:, dc * 128:(dc + 1) * 128], identb[:]),
                    rd=[zb[:], identb[:]], wr=[pbf16(bk)])
                P.op("vector", lambda eng, V, dc=dc, bk=bk: eng.tensor_copy(
                    zfm[:, dc * 128:(dc + 1) * 128], pbf16(bk, 128)),
                    rd=[pbf16(bk)], wr=[zfm[:]])
            for dc in range(32):
                P.op("tensor", lambda eng, V, dc=dc: eng.matmul(
                    pf32(2)[:, 0:E], zfm[:, dc * 128:(dc + 1) * 128],
                    WrT[:, dc * E:(dc + 1) * E],
                    start=(dc == 0), stop=(dc == 31)),
                    rd=[zfm[:], WrT[:]], wr=[pf32(2)])
            P.op("vector", lambda eng, V: eng.tensor_tensor(
                lgs[par][:], pf32(2)[:, 0:E], rbb[:], ALU.add),
                rd=[pf32(2), rbb[:]], wr=[lgs[par][:]])
            st = lambda k: stat[k][:, par:par + 1]
            P.op("vector", lambda eng, V: eng.reduce_max(st("mx"), lgs[par][:], AX.X),
                 rd=[lgs[par][:]], wr=[stat["mx"][:]])
            P.op("vector", lambda eng, V: eng.tensor_scalar(
                st("nmx"), st("mx"), -1.0, None, ALU.mult),
                rd=[stat["mx"][:]], wr=[stat["nmx"][:]])
            P.op("scalar", lambda eng, V: eng.activation(
                out=exps[par][:], in_=lgs[par][:], func=AF.Exp, bias=st("nmx"),
                accum_out=st("sexp")),
                rd=[lgs[par][:], stat["nmx"][:]], wr=[exps[par][:], stat["sexp"][:]])
            P.op("vector", lambda eng, V: eng.reciprocal(st("rse"), st("sexp")),
                 rd=[stat["sexp"][:]], wr=[stat["rse"][:]])
            P.op("vector", lambda eng, V: eng.tensor_scalar(
                scrs[par][:], exps[par][:], st("rse"), None, ALU.mult),
                rd=[exps[par][:], stat["rse"][:]], wr=[scrs[par][:]])
            P.op("vector", lambda eng, V: eng.max(out=top8[par][:], in_=scrs[par][:]),
                 rd=[scrs[par][:]], wr=[top8[par][:]])
            P.op("vector", lambda eng, V: eng.match_replace(
                out=zap[par][:], in_to_replace=top8[par][:], in_values=scrs[par][:],
                imm_value=0.0),
                rd=[top8[par][:], scrs[par][:]], wr=[zap[par][:]])
            P.op("vector", lambda eng, V: eng.tensor_sub(
                wden[par][:], scrs[par][:], zap[par][:]),
                rd=[scrs[par][:], zap[par][:]], wr=[wden[par][:]])
            P.op("vector", lambda eng, V: eng.reduce_sum(st("wsum"), top8[par][:], AX.X),
                 rd=[top8[par][:]], wr=[stat["wsum"][:]])
            P.op("scalar", lambda eng, V: eng.activation(
                out=wdb[par][:], in_=wden[par][:], func=AF.Copy),
                rd=[wden[par][:]], wr=[wdb[par][:]])
            P.op("tensor", lambda eng, V: eng.transpose(
                pbf16(3, 128)[0:E, 0:128], wdb[par][:, 0:E], identb[:]),
                rd=[wdb[par][:], identb[:]], wr=[pbf16(3)])
            P.op("vector", lambda eng, V: eng.tensor_copy(
                wdT[par][:], pbf16(3, 128)[0:E, 0:128]),
                rd=[pbf16(3)], wr=[wdT[par][:]])
            h = 0.5 if scale_half else 1.0
            for dh in range(8):
                bk = 4 + (dh % 2)
                sl = slice(dh * BLK, (dh + 1) * BLK)
                P.op("tensor", lambda eng, V, dh=dh, bk=bk, sl=sl: eng.matmul(
                    pf32(bk), wdT[par][:], ebb[0:E, sl], start=True, stop=True),
                    rd=[wdT[par][:], ebb[:]], wr=[pf32(bk)])
                P.op("vector", lambda eng, V, dh=dh, bk=bk, sl=sl:
                     eng.scalar_tensor_tensor(
                         xout[:, sl], pf32(bk), h, xin[:, sl], ALU.mult, ALU.add),
                     rd=[pf32(bk), xin[:]], wr=[xout[:]])
                P.op("vector", lambda eng, V, dh=dh, sl=sl: eng.tensor_scalar(
                    zws[par][:], zfpar[:, sl], st("wsum"), h, ALU.mult, ALU.mult),
                    rd=[zfpar[:], stat["wsum"][:]], wr=[zws[par][:]])
                P.op("vector", lambda eng, V, dh=dh, sl=sl: eng.tensor_add(
                    xout[:, sl], xout[:, sl], zws[par][:]),
                    rd=[zws[par][:]], wr=[xout[:]])

        if phase_limit >= 1:
            P.barrier()
            for t in range(8):
                par = t % 2
                tb, ts = t // 4, t % 4
                rows = slice(t * 128, (t + 1) * 128)
                dma_s("xa", xa[par][:], x0[rows, :], tag=f"x0t{t}")
                layer_norm(xa[par], par, par, True)
                ffn(par, xa[par][:], xa[par][:], zc[par])
                dma_s("xw", x1d[rows, :], xa[par][:], tag=f"x1w{t}")
                layer_norm(xa[par], 4 + par, par, False)
                for dc in range(32):
                    bk = (dc % 2)
                    dst = tb * 16384 + dc * BLK + ts * 128
                    P.op("tensor", lambda eng, V, dc=dc, bk=bk: eng.transpose(
                        pbf16(bk, 128), zb[:, dc * 128:(dc + 1) * 128],
                        identb[:]),
                        rd=[zb[:], identb[:]], wr=[pbf16(bk)])
                    P.op("scalar", lambda eng, V, dst=dst, bk=bk: eng.activation(
                        out=z2fm[:, dst:dst + 128], in_=pbf16(bk, 128), func=AF.Copy),
                        rd=[pbf16(bk)], wr=[z2fm[:]])

        if phase_limit >= 2:
            P.barrier()
            mats = [("k", wk, 0), ("q", wq, 2), ("v", wv, 1)]
            def section_ag(sec):
                P.coll(lambda eng, V, sec=sec: eng.collective_compute(
                    "AllGather", ALU.bypass, replica_groups=groups,
                    ins=[kvq_in[sec * D:(sec + 1) * D, :].opt()],
                    outs=[sec_out[sec].opt()],
                ), rd=[f"kvq_in{sec}"], wr=[sec_out[sec]], tag=f"ag{sec}")
                def extract(eng, V, sec=sec):
                    srcap = sec_out[sec].rearrange("(r q) c -> r (q c)", q=D)[
                        :, bass.ds(V["hbK"], NTOK * NTOK)]
                    dst = sec_me[sec].rearrange("(r q) c -> r (q c)", q=NTOK)
                    return eng.dma_start(out=dst, in_=srcap)
                P.dma("gpsimd", "gx", extract, rd=[sec_out[sec]],
                      wr=[f"sec_me{sec}"], tag=f"extract{sec}")
            for mname, wm, sec in mats:
                for hc in range(H):
                    spar = (hc // 2) % 2
                    if hc % 2 == 0:
                        strip_src = bass.AP(wm.tensor, hc * 128,
                                            [[D, 128], [128 * D, 32], [1, 256]])
                        P.dma("gpsimd", "gw",
                              (lambda eng, V, s=strip_src, d=wstrip[spar]:
                               eng.dma_start(out=d[:].rearrange(
                                   "p (dc c) -> p dc c", c=256), in_=s)),
                              rd=[wm], wr=[wstrip[spar][:]], tag=f"{mname}strip{hc}")
                    hoff = (hc % 2) * 128
                    for tb in range(2):
                        pk = tb
                        for dc in range(32):
                            P.op("tensor", lambda eng, V, dc=dc, pk=pk, tb=tb,
                                 spar=spar, hoff=hoff: eng.matmul(
                                     pf32(pk),
                                     wstrip[spar][:, dc * 256 + hoff:
                                                  dc * 256 + hoff + 128],
                                     z2fm[:, tb * 16384 + dc * BLK:
                                          tb * 16384 + (dc + 1) * BLK],
                                     start=(dc == 0), stop=(dc == 31)),
                                 rd=[wstrip[spar][:], z2fm[:]], wr=[pf32(pk)])
                        cpar = (hc * 2 + tb) % 2
                        csl = slice(tb * BLK, (tb + 1) * BLK)
                        if mname in ("q", "k"):
                            P.op("scalar", lambda eng, V, pk=pk, cpar=cpar:
                                 eng.activation(out=qraw[cpar][:], in_=pf32(pk),
                                                func=AF.Copy),
                                 rd=[pf32(pk)], wr=[qraw[cpar][:]])
                            P.op("vector", lambda eng, V, cpar=cpar, csl=csl:
                                 eng.tensor_tensor(rr1[cpar][:], qraw[cpar][:],
                                                   cosT[:, csl], ALU.mult),
                                 rd=[qraw[cpar][:], cosT[:]], wr=[rr1[cpar][:]])
                            P.op("vector", lambda eng, V, cpar=cpar, csl=csl:
                                 eng.tensor_tensor(rr2[cpar][0:64, :],
                                                   qraw[cpar][64:128, :],
                                                   sinT[64:128, csl], ALU.mult),
                                 rd=[qraw[cpar][:], sinT[:]], wr=[rr2[cpar][:]])
                            P.op("vector", lambda eng, V, cpar=cpar, csl=csl:
                                 eng.tensor_tensor(rr2[cpar][64:128, :],
                                                   qraw[cpar][0:64, :],
                                                   sinT[0:64, csl], ALU.mult),
                                 rd=[qraw[cpar][:], sinT[:]], wr=[rr2[cpar][:]])
                            P.op("vector", lambda eng, V, cpar=cpar:
                                 eng.tensor_add(rob[cpar][:], rr1[cpar][:],
                                                rr2[cpar][:]),
                                 rd=[rr1[cpar][:], rr2[cpar][:]], wr=[rob[cpar][:]])
                            src_t = rob[cpar]
                        else:
                            P.op("scalar", lambda eng, V, pk=pk, cpar=cpar:
                                 eng.activation(out=vtmp[cpar][:], in_=pf32(pk),
                                                func=AF.Copy),
                                 rd=[pf32(pk)], wr=[vtmp[cpar][:]])
                            src_t = vtmp[cpar]
                        dst = kvq_in[sec * D + hc * 128: sec * D + (hc + 1) * 128,
                                     tb * BLK:(tb + 1) * BLK]
                        kkey = f"kvq_in{sec}"
                        P.dma("sync", "xw",
                              (lambda eng, V, o=dst, s=src_t: eng.dma_start(
                                  out=o, in_=s[:])),
                              rd=[src_t[:]], wr=[kkey], tag=f"{mname}w{hc}_{tb}")

            for sec in (0, 2, 1):
                section_ag(sec)

        if phase_limit >= 3:
            P.barrier()

            def gather_load(sec, i, dst):
                src = bass.AP(sec_me[sec].tensor, (i * 128) * NTOK,
                              [[NTOK, 128], [NTOK * NTOK, 4], [1, NTOK]])
                P.dma("sync", "kv",
                      (lambda eng, V, s=src, d=dst: eng.dma_start(
                          out=d[:].rearrange("p (r c) -> p r c", c=NTOK), in_=s)),
                      rd=[f"sec_me{sec}"], wr=[dst[:]], tag=f"g{sec}_{i}")

            for i in range(HL):
                par = i % 2
                gather_load(0, i, Kh[par])
                gather_load(1, i, Vf[par])
                gather_load(2, i, Qh[par])
                for kc in range(32):
                    bk = kc % 2
                    P.op("tensor", lambda eng, V, kc=kc, bk=bk, par=par:
                         eng.transpose(pbf16(bk, 128),
                                       Vf[par][:, kc * 128:(kc + 1) * 128],
                                       identb[:]),
                         rd=[Vf[par][:], identb[:]], wr=[pbf16(bk)])
                    P.op("vector", lambda eng, V, kc=kc, bk=bk, par=par:
                         eng.tensor_copy(Vt[par][:, kc * 128:(kc + 1) * 128],
                                         pbf16(bk, 128)),
                         rd=[pbf16(bk)], wr=[Vt[par][:]])
                def emit_s_exp(gq, kb, qcol):
                    kcol = col_of(kb)
                    ppar = kb % 2
                    spar = kb % 2
                    for qs in range(4):
                        sbk = (2, 3, 5, 2)[qs]
                        P.op("tensor", lambda eng, V, qs=qs, sbk=sbk, par=par,
                             qcol=qcol, kcol=kcol: eng.matmul(
                                 pf32(sbk),
                                 Qh[par][:, qcol + qs * 128: qcol + qs * 128 + 128],
                                 Kh[par][:, kcol:kcol + BLK],
                                 start=True, stop=True),
                             rd=[Qh[par][:], Kh[par][:]], wr=[pf32(sbk)])
                        if kb == gq:
                            P.op("scalar", lambda eng, V, qs=qs, sbk=sbk,
                                 ppar=ppar: eng.activation(
                                     out=Pb[ppar][:, qs * BLK:(qs + 1) * BLK],
                                     in_=pf32(sbk), func=AF.Exp, scale=ISQ),
                                 rd=[pf32(sbk)], wr=[Pb[ppar][:]])
                        else:
                            P.op("scalar", lambda eng, V, qs=qs, sbk=sbk, spar=spar,
                                 ppar=ppar: eng.activation(
                                     out=Pb[ppar][:, qs * BLK:(qs + 1) * BLK],
                                     in_=pf32(sbk), func=AF.Exp, scale=ISQ,
                                     accum_out=sblk[:, spar * 4 + qs:
                                                    spar * 4 + qs + 1]),
                                 rd=[pf32(sbk)],
                                 wr=[Pb[ppar][:], f"sblk{spar}"], tag="exp")

                def emit_sums(gq, kb):
                    ppar = kb % 2
                    spar = kb % 2
                    if kb == gq:
                        P.op("vector", lambda eng, V, ppar=ppar:
                             eng.tensor_mul(Pb[ppar][:], Pb[ppar][:], dmask[:]),
                             rd=[dmask[:], Pb[ppar][:]], wr=[Pb[ppar][:]])
                        P.op("vector", lambda eng, V, ppar=ppar, spar=spar:
                             eng.tensor_reduce(
                                 sblk[:, spar * 4:spar * 4 + 4].rearrange(
                                     "p (a b) -> p a b", b=1),
                                 Pb[ppar][:].rearrange("p (a c) -> p a c", c=BLK),
                                 AX.X, ALU.add),
                             rd=[Pb[ppar][:]], wr=[f"sblk{spar}"])
                    P.op("vector", lambda eng, V, spar=spar: eng.tensor_add(
                        sums[:], sums[:], sblk[:, spar * 4:spar * 4 + 4]),
                        rd=[f"sblk{spar}"], wr=[sums[:]])

                def emit_tp(gq, kb, obk, nkc0):
                    kcol = col_of(kb)
                    ppar = kb % 2
                    for kc in range(4):
                        tbk = 6 + (kc // 2)
                        for qs in range(4):
                            P.op("tensor", lambda eng, V, qs=qs, kc=kc, tbk=tbk,
                                 ppar=ppar: eng.transpose(
                                     pbf16(tbk, 1024)[:, (kc % 2) * BLK + qs * 128:
                                                      (kc % 2) * BLK + (qs + 1) * 128],
                                     Pb[ppar][:, qs * BLK + kc * 128:
                                              qs * BLK + (kc + 1) * 128],
                                     identb[:]),
                                 rd=[Pb[ppar][:], identb[:]], wr=[pbf16(tbk)])
                    for pr in range(2):
                        P.op("vector", lambda eng, V, pr=pr:
                             eng.tensor_copy(PTs[pr][:], pbf16(6 + pr, 1024)),
                             rd=[pbf16(6 + pr)], wr=[PTs[pr][:]])
                    for kc in range(4):
                        P.op("tensor", lambda eng, V, kc=kc, kcol=kcol, obk=obk,
                             par=par, first=(nkc0 + kc == 0),
                             last=(kb == gq and kc == 3): eng.matmul(
                                 pf32(obk),
                                 Vt[par][:, kcol + kc * 128:kcol + (kc + 1) * 128],
                                 PTs[kc // 2][:, (kc % 2) * BLK:(kc % 2 + 1) * BLK],
                                 start=first, stop=last),
                             rd=[Vt[par][:], PTs[kc // 2][:]], wr=[pf32(obk)])

                for gq in range(NB):
                    qcol = col_of(gq)
                    obk = 4
                    P.op("vector", lambda eng, V: eng.memset(sums[:], 0.0),
                         wr=[sums[:]])
                    for kb in range(gq + 1):
                        emit_s_exp(gq, kb, qcol)
                        if kb > 0:
                            emit_tp(gq, kb - 1, obk, (kb - 1) * 4)
                        emit_sums(gq, kb)
                    emit_tp(gq, gq, obk, gq * 4)
                    P.op("vector", lambda eng, V: eng.reciprocal(rs4[:], sums[:]),
                         rd=[sums[:]], wr=[rs4[:]])
                    for qs in range(4):
                        P.op("tensor", lambda eng, V, qs=qs: eng.transpose(
                            pf32(6)[0:1, qs * 128:(qs + 1) * 128],
                            rs4[:, qs:qs + 1], identf[:]),
                            rd=[rs4[:], identf[:]], wr=[pf32(6)])
                    P.op("vector", lambda eng, V: eng.tensor_copy(
                        rrs[:], pf32(6)[0:1, 0:BLK]),
                        rd=[pf32(6)], wr=[rrs[:]])
                    P.op("tensor", lambda eng, V: eng.matmul(
                        pf32(6), ones1[:], rrs[:], start=True, stop=True),
                        rd=[ones1[:], rrs[:]], wr=[pf32(6)])
                    bpar = gq % 2
                    P.op("vector", lambda eng, V, bpar=bpar: eng.tensor_copy(
                        bsum[bpar][:], pf32(6)),
                        rd=[pf32(6)], wr=[bsum[bpar][:]])
                    P.op("vector", lambda eng, V, obk=obk, bpar=bpar:
                         eng.tensor_tensor(Osb[bpar][:], pf32(obk), bsum[bpar][:],
                                           ALU.mult),
                         rd=[pf32(obk), bsum[bpar][:]], wr=[Osb[bpar][:]])
                    dma_s("xw", o_in[i * 128:(i + 1) * 128, gq * BLK:(gq + 1) * BLK],
                          Osb[bpar][:], tag=f"ow{i}_{gq}")

            P.coll(lambda eng, V: eng.collective_compute(
                "AllGather", ALU.bypass, replica_groups=groups,
                ins=[o_in.opt()], outs=[o_out.opt()]),
                rd=[o_in], wr=[o_out], tag="ag_o")

        if phase_limit >= 4:
            P.barrier()
            for tb in range(2):
                def og_load(eng, V, tb=tb):
                    col = V["mycol0"] if tb == 0 else V["mycol1"]
                    src = o_out.rearrange("(h p) c -> p h c", p=128)[
                        :, :, bass.ds(col, BLK)]
                    return eng.dma_start(
                        out=Og[:].rearrange("p (h c) -> p h c", c=BLK), in_=src)
                P.dma("sync", "kv", og_load, rd=[o_out], wr=[Og[:]], tag=f"og{tb}")
                for dc in range(32):
                    spar = (dc // 2) % 2
                    if dc % 2 == 0:
                        strip_src = bass.AP(wo.tensor, dc * 128,
                                            [[D, 128], [128 * D, 32], [1, 256]])
                        P.dma("gpsimd", "gw",
                              (lambda eng, V, s=strip_src, d=wstrip[spar]:
                               eng.dma_start(out=d[:].rearrange(
                                   "p (a c) -> p a c", c=256), in_=s)),
                              rd=[wo], wr=[wstrip[spar][:]],
                              tag=f"wostrip{tb}_{dc}")
                    doff = (dc % 2) * 128
                    x1src = bass.AP(x1d.tensor, tb * BLK * D + dc * 128,
                                    [[D, 128], [128 * D, 4], [1, 128]])
                    xspar = dc % 2
                    P.dma("sync", "xa",
                          (lambda eng, V, s=x1src, d=x1sl[xspar]:
                           eng.dma_start(out=d[:].rearrange(
                               "p (t c) -> p t c", c=128), in_=s)),
                          rd=[x1d], wr=[x1sl[xspar][:]], tag=f"x1r{tb}_{dc}")
                    pk = dc % 2
                    for ad in range(32):
                        P.op("tensor", lambda eng, V, ad=ad, pk=pk, spar=spar,
                             doff=doff: eng.matmul(
                                 pf32(pk),
                                 wstrip[spar][:, ad * 256 + doff:
                                              ad * 256 + doff + 128],
                                 Og[:, ad * BLK:(ad + 1) * BLK],
                                 start=(ad == 0), stop=(ad == 31)),
                             rd=[wstrip[spar][:], Og[:]], wr=[pf32(pk)])
                    wpar = dc % 2
                    P.op("scalar", lambda eng, V, pk=pk, wpar=wpar: eng.activation(
                        out=wosb[wpar][:], in_=pf32(pk), func=AF.Copy),
                        rd=[pf32(pk)], wr=[wosb[wpar][:]])
                    xrpar = dc % 2
                    for ts in range(4):
                        tbk = (2, 3, 6, 7)[ts]
                        P.op("tensor", lambda eng, V, ts=ts, tbk=tbk, wpar=wpar:
                             eng.transpose(pf32(tbk)[:, 0:128],
                                           wosb[wpar][:, ts * 128:(ts + 1) * 128],
                                           identf[:]),
                             rd=[wosb[wpar][:], identf[:]], wr=[pf32(tbk)])
                        P.op("vector", lambda eng, V, ts=ts, tbk=tbk, xspar=xspar,
                             xrpar=xrpar: eng.scalar_tensor_tensor(
                                 x2row[xrpar][:, ts * 128:(ts + 1) * 128],
                                 pf32(tbk)[:, 0:128], 1.0,
                                 x1sl[xspar][:, ts * 128:(ts + 1) * 128],
                                 ALU.mult, ALU.add),
                             rd=[pf32(tbk), x1sl[xspar][:]], wr=[x2row[xrpar][:]])
                    x2dst = bass.AP(x2d.tensor, (tb * BLK * D) + dc * 128,
                                    [[D, 128], [128 * D, 4], [1, 128]])
                    P.dma("sync", "xw",
                          (lambda eng, V, o=x2dst, s=x2row[xrpar]: eng.dma_start(
                              out=o, in_=s[:].rearrange("p (t c) -> p t c", c=128))),
                          rd=[x2row[xrpar][:]], wr=[x2d], tag=f"x2w{tb}_{dc}")

        if phase_limit >= 5:
            P.barrier()
            for t in range(8):
                par = t % 2
                rows = slice(t * 128, (t + 1) * 128)
                dma_s("xa", xa[par][:], x2d[rows, :], tag=f"x2t{t}")
                layer_norm(xa[par], par, par, True)
                ffn(par, xa[par][:], xa[par][:], zc[par])
                dma_s("outw", out_d[rows, :], xa[par][:], tag=f"outw{t}")

        P.barrier(tag="final")

        semset = set(P.sems) | set(P.cnt.keys())
        sems = {}
        for name in sorted(semset):
            sems[name] = en(nc.semaphore(name))

        def val_hook(eng):
            pid = eng.partition_id()
            rank = pid % 4
            hbK = eng.compute_val(rank * (NTOK * NTOK))
            mycol0 = eng.compute_val(rank * BLK)
            mycol1 = eng.compute_val((7 * BLK) - rank * BLK)
            return {"hbK": hbK, "mycol0": mycol0, "mycol1": mycol1}

        P.emit_all(nc, sems, val_hook)

    return nc


def make_consts():
    identb = np.eye(128, dtype=ml_dtypes.bfloat16)
    identf = np.eye(128, dtype=np.float32)
    dmask = np.zeros((128, 4 * BLK), dtype=ml_dtypes.bfloat16)
    for qs in range(4):
        qi = np.arange(128)[:, None]
        kj = np.arange(BLK)[None, :]
        dmask[:, qs * BLK:(qs + 1) * BLK] = (kj <= qs * 128 + qi).astype(
            ml_dtypes.bfloat16)
    return identb, identf, dmask


def rope_tables(positions):
    inv = (1.0 / 10000.0 ** (np.arange(0, HD, 2, dtype=np.float32) / HD)).astype(
        np.float32)
    freqs = positions[None, :].astype(np.float32) * inv[:, None]
    cos = np.cos(freqs).astype(np.float32)
    sin = np.sin(freqs).astype(np.float32)
    cosT = np.concatenate([cos, cos], axis=0)
    sinT = np.concatenate([sin, -sin], axis=0)
    return cosT, sinT


_NC_CACHE = {}


def kernel(L0_attn_hidden_states, L0_attn_wq, L0_attn_wk, L0_attn_wv, L0_attn_wo,
           L0_attn_kq1, L0_attn_kq2, L0_attn_kk1, L0_attn_kk2, L0_attn_scalingD,
           L0_attn_interaction_k, L0_attn_scalingD2, L0_ffn_router_weight,
           L0_ffn_router_bias, L0_ffn_expert_bias):
    x = np.ascontiguousarray(np.asarray(L0_attn_hidden_states, dtype=np.float32))
    wq = np.ascontiguousarray(np.asarray(L0_attn_wq, dtype=np.float32))
    wk = np.ascontiguousarray(np.asarray(L0_attn_wk, dtype=np.float32))
    wv = np.ascontiguousarray(np.asarray(L0_attn_wv, dtype=np.float32))
    wo = np.ascontiguousarray(np.asarray(L0_attn_wo, dtype=np.float32))
    wr = np.ascontiguousarray(np.asarray(L0_ffn_router_weight, dtype=np.float32))
    rb = np.ascontiguousarray(
        np.asarray(L0_ffn_router_bias, dtype=np.float32)).reshape(1, E)
    eb = np.ascontiguousarray(np.asarray(L0_ffn_expert_bias, dtype=np.float32))

    identb, identf, dmask = make_consts()

    key = "nc8"
    if key not in _NC_CACHE:
        _NC_CACHE[key] = build_nc(n_cores=8)
    nc = _NC_CACHE[key]

    in_maps = []
    for c in range(8):
        b, e = c // 4, c % 4
        rows_a = np.arange(e * BLK, (e + 1) * BLK)
        rows_b = np.arange((7 - e) * BLK, (8 - e) * BLK)
        rows = np.concatenate([rows_a, rows_b])
        x0c = np.ascontiguousarray(x[b][rows])
        cosT, sinT = rope_tables(rows)
        in_maps.append({
            "x0": x0c, "wq": wq, "wk": wk, "wv": wv, "wo": wo,
            "wr": wr, "rb": rb, "eb": eb,
            "identb": identb, "identf": identf,
            "cosT": np.ascontiguousarray(cosT),
            "sinT": np.ascontiguousarray(sinT),
            "dmask": dmask,
        })

    res = run_bass_kernel_spmd(nc, in_maps, core_ids=list(range(8)))

    out = np.empty((2, S, D), dtype=np.float32)
    for c in range(8):
        b, e = c // 4, c % 4
        oc = res.results[c]["out"]
        out[b, e * BLK:(e + 1) * BLK] = oc[:BLK]
        out[b, (7 - e) * BLK:(8 - e) * BLK] = oc[BLK:]
    return out
